# version 02
# baseline (speedup 1.0000x reference)
import numpy as np
import ml_dtypes
import sys

import os
if os.path.isdir("/opt/trn_rl_repo") and "/opt/trn_rl_repo" not in sys.path:
    sys.path.insert(0, "/opt/trn_rl_repo")

import concourse.bass as bass
import concourse.mybir as mybir
from concourse.bass_utils import run_bass_kernel_spmd
from contextlib import ExitStack

F32 = mybir.dt.float32
BF16 = mybir.dt.bfloat16
I32 = mybir.dt.int32
I8 = mybir.dt.int8
AF = mybir.ActivationFunctionType
ALU = mybir.AluOpType
AX = mybir.AxisListType

B, T, D = 4, 2048, 2048
NT = T // 128
NKC = D // 128
HH = 1024
EPS_LN = 1e-5
EPS_RMS = 1e-6
SOFTCAP = 50.0

bf = ml_dtypes.bfloat16


class Prog:

    def __init__(self, nc, block, ctx):
        self.nc = nc
        self.block = block
        self.ctx = ctx
        self.sem = {}
        self.cnt = {}

    def new_sem(self, name):
        h = self.ctx.enter_context(self.nc.semaphore(name))
        self.sem[name] = h
        self.cnt[name] = 0
        return h

    def W(self, *names):
        return tuple((n, self.cnt[n]) for n in names)


class Eng:
    def __init__(self, prog, section):
        self._p = prog
        self._section = section

    def __getattr__(self, op):
        p = self._p
        section = self._section

        def call(*args, wait=(), inc=None, **kw):
            res = {}
            if op == "activation":
                for key in ("bias", "scale", "alpha"):
                    if key in kw and isinstance(kw[key], float):
                        kw[key] = np.float32(kw[key])

            def f(e):
                for (sname, val) in wait:
                    if val > 0:
                        e.wait_ge(p.sem[sname], val)
                r = getattr(e, op)(*args, **kw)
                if inc is not None:
                    sname, k = inc
                    if section in ("vector", "scalar") and op != "drain":
                        e.drain().then_inc(p.sem[sname], k)
                    else:
                        r.then_inc(p.sem[sname], k)
                    p.cnt[sname] += k
                res["r"] = r

            getattr(p.block, section)(f)
            return res.get("r")

        return call

    def raw(self, f):
        getattr(self._p.block, self._section)(f)


def engines(p):
    return (Eng(p, "sync"), Eng(p, "gpsimd"), Eng(p, "vector"),
            Eng(p, "scalar"), Eng(p, "tensor"))



def emit_phaseA(p, t, debug=False):
    nc = p.nc
    ctx = p.ctx
    S, G, V, A, PE = engines(p)
    E = ctx.enter_context
    W = p.W

    arena = E(nc.sbuf_tensor("a_arena", [128, 32768], BF16))
    ident = E(nc.sbuf_tensor("a_ident", [128, 128], BF16))
    mask01 = E(nc.sbuf_tensor("a_mask01", [128, 128], BF16))
    moff = E(nc.sbuf_tensor("a_moff", [128, 128], BF16))
    xt = E(nc.sbuf_tensor("a_xt", [128, 2 * 2048], F32))
    wbuf = E(nc.sbuf_tensor("a_wbuf", [128, 16 * 1024], BF16))
    wobuf = E(nc.sbuf_tensor("a_wobuf", [128, 8 * 2048], BF16))
    ln_t = E(nc.sbuf_tensor("a_lnt", [128, 2048], BF16))
    sq_t = E(nc.sbuf_tensor("a_sqt", [128, 2048], BF16))
    qt = E(nc.sbuf_tensor("a_qt", [128, 1024], BF16))
    rt = E(nc.sbuf_tensor("a_rt", [128, 1024], BF16))
    st_t = E(nc.sbuf_tensor("a_stt", [128, 2048], BF16))
    ropecr = E(nc.sbuf_tensor("a_ropecr", [1, 512], BF16))
    ropesr = E(nc.sbuf_tensor("a_ropesr", [1, 512], BF16))
    cosb = E(nc.sbuf_tensor("a_cosb", [128, 512], BF16))
    sinb = E(nc.sbuf_tensor("a_sinb", [128, 512], BF16))
    gam = E(nc.sbuf_tensor("a_gam", [1, 2048], BF16))
    gamb = E(nc.sbuf_tensor("a_gamb", [128, 2048], BF16))
    stat = E(nc.sbuf_tensor("a_stat", [128, 16], F32))
    stat2 = E(nc.sbuf_tensor("a_stat2", [128, 16], F32))
    onec = E(nc.sbuf_tensor("a_onec", [1, 128], BF16))
    o_sb = E(nc.sbuf_tensor("a_ofm", [128, 8 * 2048], BF16))

    q_h = bass.AP(arena, 0, [[32768, 128], [1, 2048]])
    k_h = bass.AP(arena, 2048, [[32768, 128], [1, 2048]])
    v_h = bass.AP(arena, 4096, [[32768, 128], [1, 2048]])
    t_buf = bass.AP(arena, 6144, [[32768, 128], [1, 2048]])
    e_buf = bass.AP(arena, 8192, [[32768, 128], [1, 2048]])
    pt_buf = bass.AP(arena, 10240, [[32768, 128], [1, 8192]])
    lnA = bass.AP(arena, 0, [[32768, 128], [1, 32768]])

    p_s = [E(nc.psum_tensor("a_ps0", [128, 512], F32)),
           E(nc.psum_tensor("a_ps1", [128, 512], F32))]
    p_t = [E(nc.psum_tensor("a_pt0", [128, 128], BF16)),
           E(nc.psum_tensor("a_pt1", [128, 128], BF16))]
    p_o = E(nc.psum_tensor("a_po", [128, 512], F32))

    for nm in ["aLD", "aPE", "aDV", "aAC", "aST", "aGP"]:
        p.new_sem(nm)

    ident_d = nc.inline_tensor(np.eye(128, dtype=bf), "identA")
    tri = np.tril(np.ones((128, 128), np.float32))
    mask01_d = nc.inline_tensor(tri.astype(bf), "mask01A")
    moff_d = nc.inline_tensor(((tri - 1.0) * 50.0).astype(bf), "moffA")
    onec_d = nc.inline_tensor(np.ones((1, 128), dtype=bf), "onecA")

    S.dma_start(out=ident[:, :], in_=ident_d.ap(), inc=("aLD", 16))
    S.dma_start(out=mask01[:, :], in_=mask01_d.ap(), inc=("aLD", 16))
    S.dma_start(out=moff[:, :], in_=moff_d.ap(), inc=("aLD", 16))
    S.dma_start(out=onec[:, :], in_=onec_d.ap(), inc=("aLD", 16))
    G.dma_start(out=ropecr[0:1, :],
                in_=t["ropech"].ap().rearrange("h f -> (h f)").unsqueeze(0), inc=("aGP", 16))
    G.dma_start(out=ropesr[0:1, :],
                in_=t["ropesh"].ap().rearrange("h f -> (h f)").unsqueeze(0), inc=("aGP", 16))
    for r in range(8):
        G.dma_start(out=gam[0:1, r * 128:(r + 1) * 128], in_=t["qnw"].ap(),
                    inc=("aGP", 16))
        G.dma_start(out=gam[0:1, 1024 + r * 128:1024 + (r + 1) * 128],
                    in_=t["knw"].ap(), inc=("aGP", 16))
    for q4 in range(4):
        PE.matmul(p_s[0][:, :], onec[0:1, :], gam[0:1, q4 * 512:(q4 + 1) * 512],
                  start=True, stop=True,
                  wait=W("aGP", "aLD", "aDV"), inc=("aPE", 1))
        V.tensor_copy(gamb[:, q4 * 512:(q4 + 1) * 512], p_s[0][:, :],
                      wait=W("aPE"), inc=("aDV", 1))

    PE.matmul(p_s[0][:, :], onec[0:1, :], ropecr[0:1, 0:512],
              start=True, stop=True, wait=W("aGP", "aDV"), inc=("aPE", 1))
    V.tensor_copy(cosb[:, :], p_s[0][:, :], wait=W("aPE"), inc=("aDV", 1))
    PE.matmul(p_s[0][:, :], onec[0:1, :], ropesr[0:1, 0:512],
              start=True, stop=True, wait=W("aDV"), inc=("aPE", 1))
    V.tensor_copy(sinb[:, :], p_s[0][:, :], wait=W("aPE"), inc=("aDV", 1))

    for tt in range(NT):
        xv = bass.AP(xt, (tt % 2) * 2048, [[4096, 128], [1, 2048]])
        S.dma_start(out=xv, in_=t["xseq"].ap()[tt * 128:(tt + 1) * 128, :],
                    wait=W("aDV", "aAC"), inc=("aLD", 16))
        V.reduce_sum(stat[:, 0:1], xv, axis=AX.X, negate=True, wait=W("aLD"))
        V.drain()
        V.tensor_scalar_mul(stat[:, 0:1], stat[:, 0:1], 1.0 / D)
        V.drain(inc=("aDV", 1))
        A.activation(sq_t[:, :], xv, AF.Square, bias=stat[:, 0:1],
                     accum_out=stat[:, 1:2], wait=W("aDV"))
        A.drain()
        A.activation(stat[:, 3:4], stat[:, 1:2], AF.Ln,
                     scale=1.0 / D, bias=EPS_LN)
        A.drain()
        A.activation(stat[:, 2:3], stat[:, 3:4], AF.Exp, scale=-0.5)
        A.drain(inc=("aAC", 1))
        V.tensor_scalar(ln_t[:, :], xv, stat[:, 0:1], stat[:, 2:3],
                        ALU.add, ALU.mult, wait=W("aAC", "aPE"), inc=("aDV", 1))
        for kc in range(NKC):
            ptx = p_t[kc % 2]
            PE.transpose(ptx[:, :], ln_t[:, kc * 128:(kc + 1) * 128],
                         ident[:, :], wait=W("aDV"), inc=("aPE", 1))
            V.tensor_copy(lnA[:, kc * 2048 + tt * 128:kc * 2048 + tt * 128 + 128],
                          ptx[:, :], wait=W("aPE"), inc=("aDV", 1))

    if debug:
        for kc in range(NKC):
            S.dma_start(out=t["dbg_ln"].ap()[kc * 128:(kc + 1) * 128, :],
                        in_=lnA[:, kc * 2048:(kc + 1) * 2048],
                        wait=W("aDV"), inc=("aST", 16))
    def project(which, wT_d, out_d, gamoff):
        G.dma_start(out=wbuf[:, :].rearrange("p (kc f) -> p kc f", f=1024),
                    in_=wT_d.ap().rearrange("(kc p) f -> p kc f", p=128),
                    wait=W("aPE"), inc=("aGP", 16))
        for tt in range(NT):
            for half in range(2):
                pb = p_s[half]
                for kc in range(NKC):
                    PE.matmul(
                        pb[:, :],
                        lnA[:, kc * 2048 + tt * 128:kc * 2048 + tt * 128 + 128],
                        wbuf[:, kc * 1024 + half * 512:kc * 1024 + half * 512 + 512],
                        start=(kc == 0), stop=(kc == NKC - 1),
                        wait=W("aGP", "aDV") if kc == 0 else (),
                        inc=("aPE", 1) if kc == NKC - 1 else None)
                V.tensor_copy(qt[:, half * 512:half * 512 + 512], pb[:, :],
                              wait=W("aPE", "aST"), inc=("aDV", 1))
            if which == "v":
                S.dma_start(out=out_d.ap()[tt * 128:(tt + 1) * 128, :],
                            in_=qt[:, :], wait=W("aDV"), inc=("aST", 16))
                continue
            V.tensor_mul(sq_t[:, 0:1024], qt[:, :], qt[:, :])
            V.reduce_sum(stat[:, 4:12],
                         sq_t[:, 0:1024].rearrange("p (h d) -> p h d", d=128),
                         axis=AX.X)
            V.drain(inc=("aDV", 1))
            A.activation(stat2[:, 0:8], stat[:, 4:12], AF.Ln,
                         scale=1.0 / 128.0, bias=EPS_RMS, wait=W("aDV"))
            A.drain()
            A.activation(stat2[:, 8:16], stat2[:, 0:8], AF.Exp, scale=-0.5)
            A.drain(inc=("aAC", 1))
            for h in range(8):
                V.tensor_scalar(qt[:, h * 128:(h + 1) * 128],
                                qt[:, h * 128:(h + 1) * 128],
                                stat2[:, 8 + h:9 + h], None, ALU.mult,
                                wait=W("aAC") if h == 0 else ())
            V.tensor_mul(qt[:, :], qt[:, :], gamb[:, gamoff:gamoff + 1024])
            V.drain(inc=("aDV", 1))
            scr = sq_t[:, 0:64].unsqueeze(2)
            for h in range(8):
                base = h * 128
                c3 = cosb[:, h * 64:(h + 1) * 64].unsqueeze(2)
                s3 = sinb[:, h * 64:(h + 1) * 64].unsqueeze(2)
                q2 = qt[:, base:base + 128].rearrange("p (f two) -> p f two", two=2)
                r2 = rt[:, base:base + 128].rearrange("p (f two) -> p f two", two=2)
                ev, od = q2[:, :, 0:1], q2[:, :, 1:2]
                rev, rod = r2[:, :, 0:1], r2[:, :, 1:2]
                V.tensor_mul(rev, ev, c3, wait=W("aPE") if h == 0 else ())
                V.tensor_mul(scr, od, s3)
                V.tensor_sub(rev, rev, scr)
                V.tensor_mul(rod, ev, s3)
                V.tensor_mul(scr, od, c3)
                V.tensor_add(rod, rod, scr)
            V.drain(inc=("aDV", 1))
            for hc in range(8):
                ptx = p_t[hc % 2]
                PE.transpose(ptx[:, :], rt[:, hc * 128:(hc + 1) * 128],
                             ident[:, :], wait=W("aDV"), inc=("aPE", 1))
                V.tensor_copy(st_t[:, hc * 128:(hc + 1) * 128], ptx[:, :],
                              wait=W("aPE", "aST"), inc=("aDV", 1))
            S.dma_start(out=out_d.ap()[:, tt * 128:(tt + 1) * 128]
                        .rearrange("(hc p) tk -> p hc tk", p=128),
                        in_=st_t[:, 0:1024].rearrange("p (hc tk) -> p hc tk", tk=128),
                        wait=W("aDV"), inc=("aST", 16))

    project("q", t["wqT"], t["qfm"], 0)
    project("k", t["wkT"], t["kfm"], 1024)
    project("v", t["wvT"], t["vtm"], 0)

    bank_guard = {0: 0, 1: 0}
    tbank_guard = {0: 0, 1: 0}
    for h in range(8):
        G.dma_start(out=q_h[0:128, :], in_=t["qfm"].ap()[h * 128:(h + 1) * 128, :],
                    wait=W("aST", "aPE", "aDV", "aAC"), inc=("aGP", 16))
        G.dma_start(out=k_h[0:128, :], in_=t["kfm"].ap()[h * 128:(h + 1) * 128, :],
                    inc=("aGP", 16))
        G.dma_start(out=v_h[:, :].rearrange("p (tkc dh) -> p tkc dh", dh=128),
                    in_=t["vtm"].ap()[:, h * 128:(h + 1) * 128]
                    .rearrange("(tkc p) dh -> p tkc dh", p=128), inc=("aGP", 16))
        ld_now = p.cnt["aGP"]
        first_pe = True
        for q4 in range(4):
            V.memset(pt_buf[:, 0:(4 * (q4 + 1)) * 512], 0.0, wait=W("aPE"))
            for ti in range(4):
                i = q4 * 4 + ti
                tq0 = i * 128
                wdt = (i + 1) * 128
                nsl = (wdt + 511) // 512
                for sl in range(nsl):
                    n = min(512, wdt - sl * 512)
                    bk = sl % 2
                    PE.matmul(p_s[bk][:, 0:n],
                              q_h[0:128, tq0:tq0 + 128],
                              k_h[0:128, sl * 512:sl * 512 + n],
                              start=True, stop=True,
                              wait=(W("aGP") if first_pe else ()) +
                                   (("aAC", bank_guard[bk]),),
                              inc=("aPE", 1))
                    first_pe = False
                    A.activation(t_buf[:, sl * 512:sl * 512 + n], p_s[bk][:, 0:n],
                                 AF.Tanh, scale=(1.0 / np.sqrt(128.0)) / SOFTCAP,
                                 wait=W("aPE", "aDV"), inc=("aAC", 1))
                    bank_guard[bk] = p.cnt["aAC"]
                dj = wdt - 128
                V.tensor_mul(t_buf[:, dj:wdt], t_buf[:, dj:wdt], mask01[:, :],
                             wait=W("aAC"))
                V.tensor_add(t_buf[:, dj:wdt], t_buf[:, dj:wdt], moff[:, :])
                V.reduce_max(stat[:, 0:1], t_buf[:, 0:wdt], axis=AX.X,
                             negate=True)
                V.drain()
                V.tensor_scalar_mul(stat[:, 0:1], stat[:, 0:1], SOFTCAP)
                V.drain(inc=("aDV", 1))
                A.activation(e_buf[:, 0:wdt], t_buf[:, 0:wdt], AF.Exp,
                             bias=stat[:, 0:1], scale=SOFTCAP,
                             accum_out=stat[:, 1:2], wait=W("aDV"))
                A.drain(inc=("aAC", 1))
                V.reciprocal(stat[:, 2:3], stat[:, 1:2], wait=W("aAC"))
                V.drain()
                V.tensor_scalar(e_buf[:, 0:wdt], e_buf[:, 0:wdt],
                                stat[:, 2:3], None, ALU.mult, inc=("aDV", 1))
                if debug and h == 0 and i in (1, 2):
                    S.dma_start(out=t["dbg_sm"].ap()[(i - 1) * 384:(i - 1) * 384 + 128, :],
                                in_=t_buf[:, 0:2048], wait=W("aDV", "aAC"), inc=("aST", 16))
                    S.dma_start(out=t["dbg_sm"].ap()[(i - 1) * 384 + 128:(i - 1) * 384 + 256, :],
                                in_=e_buf[:, 0:2048], wait=W("aDV", "aAC"), inc=("aST", 16))
                    G.dma_start(out=t["dbg_sm"].ap()[(i - 1) * 384 + 256:(i - 1) * 384 + 257, 0:32].bitcast(F32),
                                in_=stat[0:1, 0:16], wait=W("aDV", "aAC"), inc=("aGP", 16))
                    G.dma_start(out=t["dbg_sm"].ap()[(i - 1) * 384 + 257:(i - 1) * 384 + 258, 0:32].bitcast(F32),
                                in_=stat[1:2, 0:16], wait=W("aDV", "aAC"), inc=("aGP", 16))
                for tkc in range(i + 1):
                    bk = tkc % 2
                    PE.transpose(p_t[bk][:, :], e_buf[:, tkc * 128:(tkc + 1) * 128],
                                 ident[:, :],
                                 wait=W("aDV") + (("aDV", tbank_guard[bk]),),
                                 inc=("aPE", 1))
                    V.tensor_copy(pt_buf[:, tkc * 512 + ti * 128:
                                         tkc * 512 + ti * 128 + 128],
                                  p_t[bk][:, :], wait=W("aPE"), inc=("aDV", 1))
                    tbank_guard[bk] = p.cnt["aDV"]
            ntk = 4 * (q4 + 1)
            for tkc in range(ntk):
                PE.matmul(p_o[0:128, :],
                          v_h[:, tkc * 128:(tkc + 1) * 128],
                          pt_buf[:, tkc * 512:(tkc + 1) * 512],
                          start=(tkc == 0), stop=(tkc == ntk - 1),
                          wait=W("aDV", "aAC") if tkc == 0 else (),
                          inc=("aPE", 1) if tkc == ntk - 1 else None)
            A.activation(o_sb[:, h * 2048 + q4 * 512:h * 2048 + q4 * 512 + 512],
                         p_o[0:128, :], AF.Identity, wait=W("aPE"),
                         inc=("aAC", 1))

    if debug:
        for kc in range(8):
            S.dma_start(out=t["dbg_o"].ap()[kc * 128:(kc + 1) * 128, :],
                        in_=o_sb[:, kc * 2048:(kc + 1) * 2048],
                        wait=W("aAC"), inc=("aST", 16))
    G.dma_start(out=wobuf[:, :].rearrange("p (kc f) -> p kc f", f=2048),
                in_=t["woT"].ap().rearrange("(kc p) f -> p kc f", p=128),
                inc=("aGP", 16))
    for tc in range(NT):
        for dsl in range(4):
            pb = p_s[dsl % 2]
            for kc in range(8):
                PE.matmul(pb[:, :],
                          o_sb[:, kc * 2048 + tc * 128:kc * 2048 + tc * 128 + 128],
                          wobuf[:, kc * 2048 + dsl * 512:kc * 2048 + dsl * 512 + 512],
                          start=(kc == 0), stop=(kc == 7),
                          wait=W("aGP", "aAC", "aDV") if kc == 0 else (),
                          inc=("aPE", 1) if kc == 7 else None)
            V.tensor_copy(st_t[:, dsl * 512:dsl * 512 + 512], pb[:, :],
                          wait=W("aPE", "aST"), inc=("aDV", 1))
        S.dma_start(out=t["dA"].ap()[tc * 128:(tc + 1) * 128, :],
                    in_=st_t[:, :], wait=W("aDV"), inc=("aST", 16))
    S.raw(lambda s: s.wait_ge(p.sem["aST"], p.cnt["aST"]))



def build_A(debug=False):
    nc = bass.Bass("TRN2", target_bir_lowering=False)
    t = {}
    t["xseq"] = nc.dram_tensor("xseq", [T, D], F32, kind="ExternalInput")
    t["wqT"] = nc.dram_tensor("wqT", [D, HH], F32, kind="ExternalInput")
    t["wkT"] = nc.dram_tensor("wkT", [D, HH], F32, kind="ExternalInput")
    t["wvT"] = nc.dram_tensor("wvT", [D, HH], F32, kind="ExternalInput")
    t["woT"] = nc.dram_tensor("woT", [HH, D], F32, kind="ExternalInput")
    t["qnw"] = nc.dram_tensor("qnw", [1, 128], F32, kind="ExternalInput")
    t["knw"] = nc.dram_tensor("knw", [1, 128], F32, kind="ExternalInput")
    t["ropech"] = nc.dram_tensor("ropech", [8, 64], F32, kind="ExternalInput")
    t["ropesh"] = nc.dram_tensor("ropesh", [8, 64], F32, kind="ExternalInput")
    kind = "ExternalOutput" if debug else "Internal"
    t["qfm"] = nc.dram_tensor("qfm", [HH, T], BF16, kind=kind)
    t["kfm"] = nc.dram_tensor("kfm", [HH, T], BF16, kind=kind)
    t["vtm"] = nc.dram_tensor("vtm", [T, HH], BF16, kind=kind)
    t["dA"] = nc.dram_tensor("dA", [T, D], BF16, kind="ExternalOutput")
    if debug:
        t["dbg_ln"] = nc.dram_tensor("dbg_ln", [D, T], BF16, kind="ExternalOutput")
        t["dbg_sm"] = nc.dram_tensor("dbg_sm", [768, T], BF16, kind="ExternalOutput")
        t["dbg_o"] = nc.dram_tensor("dbg_o", [HH, T], BF16, kind="ExternalOutput")

    with ExitStack() as ctx:
        block = ctx.enter_context(nc.Block())
        p = Prog(nc, block, ctx)
        emit_phaseA(p, t, debug=debug)
    return nc


def phaseA_inmaps(inputs):
    wqT = np.ascontiguousarray(inputs["L0_attn_wq"].T)
    wkT = np.ascontiguousarray(inputs["L0_attn_wk"].T)
    wvT = np.ascontiguousarray(inputs["L0_attn_wv"].T)
    woT = np.ascontiguousarray(inputs["L0_attn_wo"].T)
    x = inputs["L0_attn_x"]
    maps = []
    for c in range(8):
        s, h = c // 2, c % 2
        maps.append({
            "xseq": np.ascontiguousarray(x[s], dtype=np.float32),
            "wqT": np.ascontiguousarray(wqT[:, h * HH:(h + 1) * HH]),
            "wkT": np.ascontiguousarray(wkT[:, h * HH:(h + 1) * HH]),
            "wvT": np.ascontiguousarray(wvT[:, h * HH:(h + 1) * HH]),
            "woT": np.ascontiguousarray(woT[h * HH:(h + 1) * HH, :]),
            "qnw": np.ascontiguousarray(inputs["L0_attn_q_norm_w"][None, :], dtype=np.float32),
            "knw": np.ascontiguousarray(inputs["L0_attn_k_norm_w"][None, :], dtype=np.float32),
            "ropech": np.ascontiguousarray(
                inputs["L0_attn_rope_cos"][h * 8:(h + 1) * 8, :], dtype=np.float32),
            "ropesh": np.ascontiguousarray(
                inputs["L0_attn_rope_sin"][h * 8:(h + 1) * 8, :], dtype=np.float32),
        })
    return maps


TOK = 1024
TC = TOK // 128
NE = 32
BIGR = 1.0e4


def emit_phaseB(p, t, debug=False):
    nc = p.nc
    ctx = p.ctx
    S, G, V, A, PE = engines(p)
    E = ctx.enter_context
    W = p.W

    lnB = E(nc.sbuf_tensor("b_lnB", [128, 16 * TOK], BF16))
    big = E(nc.sbuf_tensor("b_big", [128, 32768], BF16))
    xlat = bass.AP(big, 0, [[32768, 128], [1, 8192]])
    hP = bass.AP(big, 8192, [[32768, 128], [1, 8192]])
    w1b = bass.AP(big, 16384, [[32768, 128], [1, 8192]])
    w2b = bass.AP(big, 24576, [[32768, 128], [1, 8192]])
    bigf = big.bitcast(F32)
    accv = bass.AP(bigf, 0, [[16384, 128], [1, 16384]])
    ylat = E(nc.sbuf_tensor("b_ylat", [128, 8 * TOK], F32))
    ylatb = E(nc.sbuf_tensor("b_ylatb", [128, 8 * TOK], BF16))
    wdb = E(nc.sbuf_tensor("b_wdb", [128, 8 * 2048], BF16))
    supw = E(nc.sbuf_tensor("b_supw", [128, 16 * 512], BF16))
    supf = E(nc.sbuf_tensor("b_supf", [128, 4 * TOK], BF16))
    gwb = E(nc.sbuf_tensor("b_gwb", [128, 16 * 32], BF16))
    ylatbf = ylat.bitcast(BF16)
    xt = bass.AP(ylat, 0, [[8192, 128], [1, 2048]])
    h1t = bass.AP(ylat, 2048, [[8192, 128], [1, 2048]])
    da0t = bass.AP(ylatbf, 8192, [[16384, 128], [1, 2048]])
    da1t = bass.AP(ylatbf, 10240, [[16384, 128], [1, 2048]])
    ln_t = bass.AP(ylatbf, 12288, [[16384, 128], [1, 2048]])
    sq_t = bass.AP(ylatbf, 14336, [[16384, 128], [1, 2048]])
    stat = E(nc.sbuf_tensor("b_stat", [128, 16], F32))
    logi = E(nc.sbuf_tensor("b_logi", [128, TC * 32], F32))
    scor = E(nc.sbuf_tensor("b_scor", [128, TC * 32], F32))
    cw = E(nc.sbuf_tensor("b_cw", [128, TC * 32], F32))
    scw = E(nc.sbuf_tensor("b_scw", [128, TC * 32], BF16))
    scwT = E(nc.sbuf_tensor("b_scwT", [32, TOK], BF16))
    crow = bass.AP(supw, 0, [[8192, 1], [1, TOK]])
    cwb = bass.AP(supw, 1024, [[8192, 128], [1, TOK]])
    gbb = E(nc.sbuf_tensor("b_gbb", [128, 32], BF16))
    gbrow = E(nc.sbuf_tensor("b_gbrow", [1, 32], BF16))
    onec = E(nc.sbuf_tensor("b_onec", [1, 128], BF16))
    tmpA = bass.AP(gwb, 0, [[512, 128], [1, 512]])
    tmpB = bass.AP(logi.bitcast(BF16), 0, [[512, 128], [1, 512]])
    msk = E(nc.sbuf_tensor("b_msk", [128, 32], F32))
    msk0 = E(nc.sbuf_tensor("b_msk0", [128, 32], F32))
    grp = E(nc.sbuf_tensor("b_grp", [128, 16], F32))
    ident = E(nc.sbuf_tensor("b_ident", [128, 128], BF16))

    pA = E(nc.psum_tensor("b_pA", [128, 1024], F32))
    pB = E(nc.psum_tensor("b_pB", [128, 1024], F32))
    pG = E(nc.psum_tensor("b_pG", [128, 32], F32))
    pT = E(nc.psum_tensor("b_pT", [32, 128], BF16))
    pC = E(nc.psum_tensor("b_pC", [128, 1024], F32))

    for nm in ["bLD", "bPE", "bDV", "bAC", "bST", "bGP"]:
        p.new_sem(nm)

    ident_d = nc.inline_tensor(np.eye(128, dtype=bf), "identB")
    onec_d = nc.inline_tensor(np.ones((1, 128), dtype=bf), "onecB")
    S.dma_start(out=ident[:, :], in_=ident_d.ap(), inc=("bLD", 16))
    S.dma_start(out=onec[:, :], in_=onec_d.ap(), inc=("bLD", 16))
    G.dma_start(out=gwb[:, :].rearrange("p (kc e) -> p kc e", e=32),
                in_=t["gatewT"].ap().rearrange("(kc p) e -> p kc e", p=128),
                inc=("bGP", 16))
    G.dma_start(out=gbrow[:, :], in_=t["gateb"].ap(), inc=("bGP", 16))
    PE.matmul(pG[:, :], onec[0:1, :], gbrow[0:1, :], start=True, stop=True,
              wait=W("bGP", "bLD"), inc=("bPE", 1))
    V.tensor_copy(gbb[:, :], pG[:, :], wait=W("bPE"), inc=("bDV", 1))

    for tc in range(TC):
        S.dma_start(out=xt[:, :], in_=t["xsl"].ap()[tc * 128:(tc + 1) * 128, :],
                    wait=W("bDV", "bAC", "bST"), inc=("bLD", 16))
        S.dma_start(out=da0t[:, :], in_=t["dA0"].ap()[tc * 128:(tc + 1) * 128, :],
                    inc=("bLD", 16))
        S.dma_start(out=da1t[:, :], in_=t["dA1"].ap()[tc * 128:(tc + 1) * 128, :],
                    inc=("bLD", 16))
        V.tensor_add(h1t[:, :], xt[:, :], da0t[:, :], wait=W("bLD"))
        V.tensor_add(h1t[:, :], h1t[:, :], da1t[:, :], inc=("bDV", 1))
        S.dma_start(out=t["h1d"].ap()[tc * 128:(tc + 1) * 128, :],
                    in_=h1t[:, :], wait=W("bDV"), inc=("bST", 16))
        V.reduce_sum(stat[:, 0:1], h1t[:, :], axis=AX.X, negate=True)
        V.drain()
        V.tensor_scalar_mul(stat[:, 0:1], stat[:, 0:1], 1.0 / D)
        V.drain(inc=("bDV", 1))
        A.activation(sq_t[:, :], h1t[:, :], AF.Square, bias=stat[:, 0:1],
                     accum_out=stat[:, 1:2], wait=W("bDV"))
        A.drain()
        A.activation(stat[:, 3:4], stat[:, 1:2], AF.Ln, scale=1.0 / D, bias=EPS_LN)
        A.drain()
        A.activation(stat[:, 2:3], stat[:, 3:4], AF.Exp, scale=-0.5)
        A.drain(inc=("bAC", 1))
        V.tensor_scalar(ln_t[:, :], h1t[:, :], stat[:, 0:1], stat[:, 2:3],
                        ALU.add, ALU.mult, wait=W("bAC"), inc=("bDV", 1))
        for kc in range(NKC):
            bkk = kc % 2
            pb = [pA, pB][bkk]
            PE.transpose(pb.bitcast(BF16)[0:128, 0:128], ln_t[:, kc * 128:(kc + 1) * 128],
                         ident[:, :], wait=W("bDV"), inc=("bPE", 1))
            V.tensor_copy(lnB[:, kc * TOK + tc * 128:kc * TOK + tc * 128 + 128],
                          pb.bitcast(BF16)[0:128, 0:128], wait=W("bPE"),
                          inc=("bDV", 1))

    for tc in range(TC):
        for kc in range(NKC):
            PE.matmul(pG[:, :],
                      lnB[:, kc * TOK + tc * 128:kc * TOK + tc * 128 + 128],
                      gwb[:, kc * 32:(kc + 1) * 32],
                      start=(kc == 0), stop=(kc == NKC - 1),
                      wait=W("bDV") if kc == 0 else (),
                      inc=("bPE", 1) if kc == NKC - 1 else None)
        V.tensor_copy(logi[:, tc * 32:(tc + 1) * 32], pG[:, :], wait=W("bPE"),
                      inc=("bDV", 1))
    A.activation(scor[:, :], logi[:, :], AF.Sigmoid, wait=W("bDV"))
    A.drain(inc=("bAC", 1))
    emit_routing(p, t, scor, msk, msk0, grp, cw, stat, gbb, debug)

    A.activation(tmpA[:, 0:TC * 32], cw[:, :], AF.Ln, bias=1e-30, wait=W("bDV"))
    A.drain()
    A.activation(scw[:, :], tmpA[:, 0:TC * 32], AF.Exp, scale=0.5)
    A.drain(inc=("bAC", 1))
    for tc in range(TC):
        PE.transpose(pT[0:32, :], scw[:, tc * 32:(tc + 1) * 32], ident[:, :],
                     wait=W("bAC") + W("bDV"), inc=("bPE", 1))
        V.tensor_copy(scwT[0:32, tc * 128:(tc + 1) * 128], pT[0:32, :],
                      wait=W("bPE"), inc=("bDV", 1))

    fc1buf = bass.AP(big, 16384, [[32768, 128], [1, 16384]])
    G.dma_start(out=fc1buf[:, :].rearrange("p (kc f) -> p kc f", f=1024),
                in_=t["fc1T"].ap().rearrange("(kc p) f -> p kc f", p=128),
                wait=W("bPE"), inc=("bGP", 16))
    for lc in range(8):
        for th in range(2):
            pb = [pA, pB][th]
            for kc in range(NKC):
                PE.matmul(pb[:, 0:512],
                          fc1buf[:, kc * 1024 + lc * 128:kc * 1024 + lc * 128 + 128],
                          lnB[:, kc * TOK + th * 512:kc * TOK + th * 512 + 512],
                          start=(kc == 0), stop=(kc == NKC - 1),
                          wait=W("bGP", "bDV") if kc == 0 else (),
                          inc=("bPE", 1) if kc == NKC - 1 else None)
            V.tensor_copy(xlat[:, lc * TOK + th * 512:lc * TOK + th * 512 + 512],
                          pb[:, 0:512], wait=W("bPE"), inc=("bDV", 1))

    for e in range(NE):
        G.dma_start(out=w1b[:, :].rearrange("p (kc f) -> p kc f", f=1024),
                    in_=t["w1T"].ap()[e].rearrange("(kc p) f -> p kc f", p=128),
                    wait=W("bPE"), inc=("bGP", 16))
        G.dma_start(out=w2b[:, :].rearrange("p (kc f) -> p kc f", f=1024),
                    in_=t["w2T"].ap()[e].rearrange("(kc p) f -> p kc f", p=128),
                    inc=("bGP", 16))
        G.dma_start(out=crow[0:1, :], in_=scwT[e:e + 1, :], wait=W("bPE"),
                    inc=("bGP", 16))
        for th in range(2):
            PE.matmul(pC[:, th * 512:th * 512 + 512], onec[0:1, :],
                      crow[0:1, th * 512:th * 512 + 512], start=True, stop=True,
                      wait=W("bGP", "bDV") if th == 0 else (), inc=("bPE", 1))
        V.tensor_copy(cwb[:, :], pC[:, :], wait=W("bPE"), inc=("bDV", 1))
        for ic in range(8):
            pb = [pA, pB][ic % 2]
            for th in range(2):
                for kc in range(8):
                    PE.matmul(pb[:, th * 512:th * 512 + 512],
                              w1b[:, kc * 1024 + ic * 128:kc * 1024 + ic * 128 + 128],
                              xlat[:, kc * TOK + th * 512:kc * TOK + th * 512 + 512],
                              start=(kc == 0), stop=(kc == 7),
                              wait=W("bDV") if (th == 0 and kc == 0) else (),
                              inc=("bPE", 1) if kc == 7 else None)
            for th in range(2):
                V.tensor_relu(tmpA[:, :], pb[:, th * 512:th * 512 + 512],
                              wait=W("bPE", "bAC"))
                V.tensor_mul(tmpB[:, :], tmpA[:, :], cwb[:, th * 512:th * 512 + 512])
                V.drain(inc=("bDV", 1))
                A.activation(hP[:, ic * TOK + th * 512:ic * TOK + th * 512 + 512],
                             tmpB[:, :], AF.Square, wait=W("bDV"), inc=("bAC", 1))
        for lc in range(8):
            pb = [pA, pB][lc % 2]
            for th in range(2):
                for ic in range(8):
                    PE.matmul(pb[:, th * 512:th * 512 + 512],
                              w2b[:, ic * 1024 + lc * 128:ic * 1024 + lc * 128 + 128],
                              hP[:, ic * TOK + th * 512:ic * TOK + th * 512 + 512],
                              start=(ic == 0), stop=(ic == 7),
                              wait=W("bAC", "bDV") if (th == 0 and ic == 0) else (),
                              inc=("bPE", 1) if ic == 7 else None)
            if e == 0:
                V.tensor_copy(ylat[:, lc * TOK:(lc + 1) * TOK], pb[:, :],
                              wait=W("bPE"), inc=("bDV", 1))
            else:
                V.tensor_add(ylat[:, lc * TOK:(lc + 1) * TOK],
                             ylat[:, lc * TOK:(lc + 1) * TOK], pb[:, :],
                             wait=W("bPE"), inc=("bDV", 1))

    for lc in range(8):
        V.tensor_copy(ylatb[:, lc * TOK:(lc + 1) * TOK],
                      ylat[:, lc * TOK:(lc + 1) * TOK], inc=("bDV", 1))

    G.dma_start(out=wdb[:, :].rearrange("p (kc f) -> p kc f", f=2048),
                in_=t["fc2T"].ap().rearrange("(kc p) f -> p kc f", p=128),
                wait=W("bPE"), inc=("bGP", 16))
    for tc in range(TC):
        for dsl in range(4):
            pb = [pA, pB][dsl % 2]
            for kc in range(8):
                PE.matmul(pb[:, 0:512],
                          ylatb[:, kc * TOK + tc * 128:kc * TOK + tc * 128 + 128],
                          wdb[:, kc * 2048 + dsl * 512:kc * 2048 + dsl * 512 + 512],
                          start=(kc == 0), stop=(kc == 7),
                          wait=W("bGP", "bDV") if kc == 0 else (),
                          inc=("bPE", 1) if kc == 7 else None)
            V.tensor_copy(accv[:, tc * 2048 + dsl * 512:tc * 2048 + dsl * 512 + 512],
                          pb[:, 0:512], wait=W("bPE"), inc=("bDV", 1))

    for g in range(8):
        G.dma_start(out=supw[:, :].rearrange("p (kc f) -> p kc f", f=512),
                    in_=t["shupT"].ap()[:, g * 512:(g + 1) * 512]
                    .rearrange("(kc p) f -> p kc f", p=128),
                    wait=W("bPE", "bDV", "bAC"), inc=("bGP", 16))
        G.dma_start(out=wdb[:, 0:4 * 2048].rearrange("p (kc f) -> p kc f", f=2048),
                    in_=t["shdnT"].ap()[g * 512:(g + 1) * 512, :]
                    .rearrange("(kc p) f -> p kc f", p=128),
                    inc=("bGP", 16))
        for sc in range(4):
            pb = [pA, pB][sc % 2]
            for th in range(2):
                for kc in range(NKC):
                    PE.matmul(pb[:, th * 512:th * 512 + 512],
                              supw[:, kc * 512 + sc * 128:kc * 512 + sc * 128 + 128],
                              lnB[:, kc * TOK + th * 512:kc * TOK + th * 512 + 512],
                              start=(kc == 0), stop=(kc == NKC - 1),
                              wait=W("bGP", "bDV") if (th == 0 and kc == 0) else (),
                              inc=("bPE", 1) if kc == NKC - 1 else None)
            for th in range(2):
                V.tensor_relu(tmpA[:, :], pb[:, th * 512:th * 512 + 512],
                              wait=W("bPE", "bAC"))
                V.drain(inc=("bDV", 1))
                A.activation(supf[:, sc * TOK + th * 512:sc * TOK + th * 512 + 512],
                             tmpA[:, :], AF.Square, wait=W("bDV"), inc=("bAC", 1))
        for tc in range(TC):
            for dsl in range(4):
                pb = [pA, pB][dsl % 2]
                for sc in range(4):
                    PE.matmul(pb[:, 0:512],
                              supf[:, sc * TOK + tc * 128:sc * TOK + tc * 128 + 128],
                              wdb[:, sc * 2048 + dsl * 512:sc * 2048 + dsl * 512 + 512],
                              start=(sc == 0), stop=(sc == 3),
                              wait=W("bAC", "bDV") if (sc == 0) else (),
                              inc=("bPE", 1) if sc == 3 else None)
                V.tensor_add(accv[:, tc * 2048 + dsl * 512:tc * 2048 + dsl * 512 + 512],
                             accv[:, tc * 2048 + dsl * 512:tc * 2048 + dsl * 512 + 512],
                             pb[:, 0:512], wait=W("bPE"), inc=("bDV", 1))

    for tc in range(TC):
        S.dma_start(out=h1t[:, :], in_=t["h1d"].ap()[tc * 128:(tc + 1) * 128, :],
                    wait=W("bDV", "bST"), inc=("bLD", 16))
        V.tensor_add(h1t[:, :], h1t[:, :], accv[:, tc * 2048:(tc + 1) * 2048],
                     wait=W("bLD"), inc=("bDV", 1))
        S.dma_start(out=t["h2"].ap()[tc * 128:(tc + 1) * 128, :], in_=h1t[:, :],
                    wait=W("bDV"), inc=("bST", 16))
    if debug:
        for tc in range(TC):
            S.dma_start(out=t["dbg_cw"].ap()[tc * 128:(tc + 1) * 128, :],
                        in_=cw[:, tc * 32:(tc + 1) * 32], inc=("bST", 16))
        for lc in range(8):
            S.dma_start(out=t["dbg_ylat"].ap()[lc * 128:(lc + 1) * 128, :],
                        in_=ylat[:, lc * TOK:(lc + 1) * TOK], inc=("bST", 16))
    S.raw(lambda s: s.wait_ge(p.sem["bST"], p.cnt["bST"]))


def emit_routing(p, t, scor, msk, msk0, grp, cw, stat, gbb, debug):
    nc = p.nc
    S, G, V, A, PE = engines(p)
    W = p.W
    for tc in range(TC):
        sc_v = scor[:, tc * 32:(tc + 1) * 32]
        V.tensor_add(msk0[:, :], sc_v, gbb[:, :], wait=W("bAC"))
        V.drain()
        for g in range(4):
            gsl = msk0[:, g * 8:(g + 1) * 8]
            V.reduce_max(grp[:, 8 + g:9 + g], gsl, axis=AX.X)
            V.drain()
            V.tensor_scalar(msk[:, g * 8:(g + 1) * 8], gsl,
                            grp[:, 8 + g:9 + g], None, ALU.is_ge)
            V.drain()
            V.scalar_tensor_tensor(msk[:, g * 8:(g + 1) * 8],
                                   msk[:, g * 8:(g + 1) * 8], -BIGR, gsl,
                                   ALU.mult, ALU.add)
            V.reduce_max(grp[:, 12 + g:13 + g], msk[:, g * 8:(g + 1) * 8],
                         axis=AX.X)
            V.drain()
            V.tensor_add(grp[:, g:g + 1], grp[:, 8 + g:9 + g],
                         grp[:, 12 + g:13 + g])
            V.drain()
        V.reduce_max(stat[:, 4:5], grp[:, 0:4], axis=AX.X)
        V.drain()
        V.tensor_scalar(grp[:, 4:8], grp[:, 0:4], stat[:, 4:5], None, ALU.is_ge)
        V.drain()
        V.scalar_tensor_tensor(grp[:, 4:8], grp[:, 4:8], -BIGR, grp[:, 0:4],
                               ALU.mult, ALU.add)
        V.reduce_max(stat[:, 5:6], grp[:, 4:8], axis=AX.X)
        V.drain()
        V.tensor_scalar(grp[:, 4:8], grp[:, 0:4], stat[:, 5:6], None, ALU.is_ge)
        V.drain()
        V.tensor_scalar(grp[:, 4:8], grp[:, 4:8], 1.0, BIGR,
                        ALU.subtract, ALU.mult)
        V.drain()
        for g in range(4):
            V.tensor_scalar(msk0[:, g * 8:(g + 1) * 8], msk0[:, g * 8:(g + 1) * 8],
                            grp[:, 4 + g:5 + g], None, ALU.add)
        V.drain()
        V.tensor_copy(msk[:, :], msk0[:, :])
        V.drain()
        for j in range(8):
            V.reduce_max(stat[:, 6:7], msk[:, :], axis=AX.X)
            V.drain()
            if j < 7:
                V.tensor_scalar(cw[:, tc * 32:(tc + 1) * 32], msk[:, :],
                                stat[:, 6:7], None, ALU.is_ge)
                V.drain()
                V.scalar_tensor_tensor(msk[:, :], cw[:, tc * 32:(tc + 1) * 32],
                                       -BIGR, msk[:, :], ALU.mult, ALU.add)
                V.drain()
        V.tensor_scalar(msk[:, :], msk0[:, :], stat[:, 6:7], None, ALU.is_ge)
        V.drain()
        V.tensor_mul(msk[:, :], msk[:, :], sc_v)
        V.drain()
        V.reduce_sum(stat[:, 7:8], msk[:, :], axis=AX.X)
        V.drain()
        V.reciprocal(stat[:, 8:9], stat[:, 7:8])
        V.drain()
        V.tensor_scalar(cw[:, tc * 32:(tc + 1) * 32], msk[:, :],
                        stat[:, 8:9], 2.5, ALU.mult, ALU.mult)
        V.drain(inc=("bDV", 1))


def build_B(debug=False):
    nc = bass.Bass("TRN2", target_bir_lowering=False)
    t = {}
    t["xsl"] = nc.dram_tensor("xsl", [TOK, D], F32, kind="ExternalInput")
    t["dA0"] = nc.dram_tensor("dA0", [TOK, D], BF16, kind="ExternalInput")
    t["dA1"] = nc.dram_tensor("dA1", [TOK, D], BF16, kind="ExternalInput")
    t["gatewT"] = nc.dram_tensor("gatewT", [D, 32], F32, kind="ExternalInput")
    t["gateb"] = nc.dram_tensor("gateb", [1, 32], F32, kind="ExternalInput")
    t["fc1T"] = nc.dram_tensor("fc1T", [D, 1024], F32, kind="ExternalInput")
    t["fc2T"] = nc.dram_tensor("fc2T", [1024, D], F32, kind="ExternalInput")
    t["w1T"] = nc.dram_tensor("w1T", [NE, 1024, 1024], F32, kind="ExternalInput")
    t["w2T"] = nc.dram_tensor("w2T", [NE, 1024, 1024], F32, kind="ExternalInput")
    t["shupT"] = nc.dram_tensor("shupT", [D, 4096], F32, kind="ExternalInput")
    t["shdnT"] = nc.dram_tensor("shdnT", [4096, D], F32, kind="ExternalInput")
    t["h1d"] = nc.dram_tensor("h1d", [TOK, D], F32)
    t["h2"] = nc.dram_tensor("h2", [TOK, D], F32, kind="ExternalOutput")
    if debug:
        t["dbg_cw"] = nc.dram_tensor("dbg_cw", [TOK, 32], F32, kind="ExternalOutput")
        t["dbg_ylat"] = nc.dram_tensor("dbg_ylat", [1024, TOK], F32, kind="ExternalOutput")

    with ExitStack() as ctx:
        block = ctx.enter_context(nc.Block())
        p = Prog(nc, block, ctx)
        emit_phaseB(p, t, debug=debug)
    return nc


def phaseB_inmaps(inputs, dA_by_core):
    gatewT = np.ascontiguousarray(inputs["L0_ffn_gate_w"].T)
    fc1T = np.ascontiguousarray(inputs["L0_ffn_fc1_w"].T)
    fc2T = np.ascontiguousarray(inputs["L0_ffn_fc2_w"].T)
    w1T = np.ascontiguousarray(inputs["L0_ffn_w1"].transpose(0, 2, 1))
    w2T = np.ascontiguousarray(inputs["L0_ffn_w2"].transpose(0, 2, 1))
    shupT = np.ascontiguousarray(inputs["L0_ffn_shared_up_w"].T)
    shdnT = np.ascontiguousarray(inputs["L0_ffn_shared_down_w"].T)
    x = inputs["L0_attn_x"]
    maps = []
    for c in range(8):
        s, h = c // 2, c % 2
        rows = slice(h * TOK, (h + 1) * TOK)
        maps.append({
            "xsl": np.ascontiguousarray(x[s][rows], dtype=np.float32),
            "dA0": np.ascontiguousarray(dA_by_core[2 * s][rows]),
            "dA1": np.ascontiguousarray(dA_by_core[2 * s + 1][rows]),
            "gatewT": gatewT, "gateb": np.ascontiguousarray(
                inputs["L0_ffn_gate_bias"][None, :], dtype=np.float32),
            "fc1T": fc1T, "fc2T": fc2T, "w1T": w1T, "w2T": w2T,
            "shupT": shupT, "shdnT": shdnT,
        })
    return maps


BIGNEG = -1.0e38


def emit_quant(p, src, dst, nb, qs, stat_off=0):
    S, G, V, A, PE = engines(p)
    W = p.W
    wdt = nb * 16
    qa, qb, qc, qd = qs["qa"], qs["qb"], qs["qc"], qs["qd"]
    ni = qs["ni"]
    bm, sc, ri = qs["bm"], qs["sc"], qs["ri"]

    def b3(ap2d):
        return bass.AP(ap2d.tensor, ap2d.offset,
                       [list(ap2d.ap[0]), [1, nb], [0, 16]])

    def t3(ap2d):
        return bass.AP(ap2d.tensor, ap2d.offset,
                       [list(ap2d.ap[0]), [16, nb], [1, 16]])

    A.activation(qa[:, 0:wdt], src, AF.Abs, wait=W("qDV"), inc=("qAC", 1))
    V.reduce_max(bm[:, 0:nb],
                 bass.AP(qa.tensor, qa.offset, [list(qa.ap[0]), [16, nb], [1, 16]]),
                 axis=AX.X, wait=W("qAC"))
    V.drain()
    V.tensor_scalar(sc[:, 0:nb], bm[:, 0:nb], 1.0 / 6.0, 1e-30, ALU.mult, ALU.max)
    V.drain()
    V.reciprocal(ri[:, 0:nb], sc[:, 0:nb])
    V.drain()
    V.tensor_tensor(t3(qb[:, 0:wdt]), t3(qa[:, 0:wdt]), b3(ri[:, 0:nb]), ALU.mult,
                    inc=("qDV", 1))
    A.activation(qc[:, 0:wdt], qb[:, 0:wdt], AF.Relu, bias=-2.0, wait=W("qDV"))
    A.activation(qd[:, 0:wdt], qb[:, 0:wdt], AF.Relu, bias=-2.0, scale=0.5,
                 inc=("qAC", 1))
    V.scalar_tensor_tensor(qb[:, 0:wdt], qb[:, 0:wdt], 2.0, qc[:, 0:wdt],
                           ALU.mult, ALU.subtract, wait=W("qAC"))
    V.tensor_sub(qb[:, 0:wdt], qb[:, 0:wdt], qd[:, 0:wdt])
    V.tensor_scalar_add(ni[:, 0:wdt], qb[:, 0:wdt], 0.0)
    V.tensor_copy(qc[:, 0:wdt], ni[:, 0:wdt], inc=("qDV", 1))
    A.activation(qd[:, 0:wdt], qc[:, 0:wdt], AF.Relu, bias=-4.0, wait=W("qDV"),
                 inc=("qAC", 1))
    V.tensor_add(qd[:, 0:wdt], qc[:, 0:wdt], qd[:, 0:wdt], wait=W("qAC"),
                 inc=("qDV", 1))
    A.activation(qb[:, 0:wdt], qc[:, 0:wdt], AF.Relu, bias=-6.0)
    A.activation(qc[:, 0:wdt], src, AF.Sign, wait=W("qDV"), inc=("qAC", 1))
    V.scalar_tensor_tensor(qd[:, 0:wdt], qd[:, 0:wdt], 0.5, qb[:, 0:wdt],
                           ALU.mult, ALU.add, wait=W("qAC"))
    V.tensor_mul(qd[:, 0:wdt], qd[:, 0:wdt], qc[:, 0:wdt])
    V.tensor_tensor(t3(dst), t3(qd[:, 0:wdt]), b3(sc[:, 0:nb]), ALU.mult,
                    inc=("qDV", 1))


def emit_phaseC(p, t, debug=False, stages=9):
    nc = p.nc
    ctx = p.ctx
    S, G, V, A, PE = engines(p)
    E = ctx.enter_context
    W = p.W

    arena = E(nc.sbuf_tensor("c_arena", [128, 32768], BF16))
    o_sb = E(nc.sbuf_tensor("c_ofm", [128, 8 * 2048], BF16))
    wq_sb = E(nc.sbuf_tensor("c_wq", [128, 16 * 1024], BF16))
    ident = E(nc.sbuf_tensor("c_ident", [128, 128], BF16))
    mask01 = E(nc.sbuf_tensor("c_mask01", [128, 128], BF16))
    moff = E(nc.sbuf_tensor("c_moff", [128, 128], BF16))
    xt = E(nc.sbuf_tensor("c_xt", [128, 2048], F32))
    ln_t = E(nc.sbuf_tensor("c_lnt", [128, 2048], F32))
    qsa = E(nc.sbuf_tensor("c_qsa", [128, 2048], F32))
    qsb = E(nc.sbuf_tensor("c_qsb", [128, 2048], F32))
    qsc = E(nc.sbuf_tensor("c_qsc", [128, 2048], F32))
    qsd = E(nc.sbuf_tensor("c_qsd", [128, 2048], F32))
    nibuf = E(nc.sbuf_tensor("c_ni", [128, 2048], I32))
    bmb = E(nc.sbuf_tensor("c_bm", [128, 128], F32))
    scb = E(nc.sbuf_tensor("c_sc", [128, 128], F32))
    rib = E(nc.sbuf_tensor("c_ri", [128, 128], F32))
    qt = E(nc.sbuf_tensor("c_qt", [128, 1024], BF16))
    rt = E(nc.sbuf_tensor("c_rt", [128, 1024], BF16))
    st_t = E(nc.sbuf_tensor("c_stt", [128, 2048], BF16))
    ropec = E(nc.sbuf_tensor("c_ropec", [128, 16 * 32], BF16))
    ropes = E(nc.sbuf_tensor("c_ropes", [128, 16 * 32], BF16))
    stat = E(nc.sbuf_tensor("c_stat", [128, 16], F32))

    arf = arena.bitcast(F32)
    q_h = bass.AP(arf, 0, [[16384, 128], [1, 2048]])
    k_h = bass.AP(arf, 2048, [[16384, 128], [1, 2048]])
    v_h = bass.AP(arena, 8192, [[32768, 128], [1, 2048]])
    t_buf = bass.AP(arf, 5120, [[16384, 128], [1, 2048]])
    e_buf = bass.AP(arena, 14336, [[32768, 128], [1, 2048]])
    pt_buf = bass.AP(arena, 16384, [[32768, 128], [1, 8192]])
    xqfm = bass.AP(arena, 0, [[32768, 128], [1, 32768]])

    p_s = [E(nc.psum_tensor("c_ps0", [128, 512], F32)),
           E(nc.psum_tensor("c_ps1", [128, 512], F32))]
    p_t = [E(nc.psum_tensor("c_pt0", [128, 128], BF16)),
           E(nc.psum_tensor("c_pt1", [128, 128], BF16))]
    p_o = E(nc.psum_tensor("c_po", [128, 512], F32))
    p_tf = E(nc.psum_tensor("c_ptf", [128, 128], F32))
    p_tf2 = E(nc.psum_tensor("c_ptf2", [128, 128], F32))
    oh32 = E(nc.sbuf_tensor("c_oh32", [128, 2048], F32))
    identf = E(nc.sbuf_tensor("c_identf", [128, 128], F32))


    for nm in ["cLD", "cPE", "cDV", "cAC", "cST", "cGP", "qDV", "qAC"]:
        p.new_sem(nm)
    qs = {"qa": qsa[:, :], "qb": qsb[:, :], "qc": qsc[:, :], "qd": qsd[:, :],
          "ni": nibuf[:, :], "bm": bmb[:, :], "sc": scb[:, :], "ri": rib[:, :]}

    ident_d = nc.inline_tensor(np.eye(128, dtype=bf), "identC")
    tri = np.tril(np.ones((128, 128), np.float32))
    mask01_d = nc.inline_tensor(tri.astype(bf), "mask01C")
    moff_d = nc.inline_tensor(((tri - 1.0) * 1.0).astype(bf), "moffC")
    invf = 1.0 / 10000.0 ** (np.arange(0, 64, 2, dtype=np.float32) / 64)
    ang = np.outer(np.arange(T, dtype=np.float32), invf)
    ropec_d = nc.inline_tensor(np.cos(ang).astype(bf), "ropecC")
    ropes_d = nc.inline_tensor(np.sin(ang).astype(bf), "ropesC")
    identf_d = nc.inline_tensor(np.eye(128, dtype=np.float32), "identfC")

    S.dma_start(out=ident[:, :], in_=ident_d.ap(), inc=("cLD", 16))
    S.dma_start(out=mask01[:, :], in_=mask01_d.ap(), inc=("cLD", 16))
    S.dma_start(out=identf[:, :], in_=identf_d.ap(), inc=("cLD", 16))
    S.dma_start(out=moff[:, :], in_=moff_d.ap(), inc=("cLD", 16))
    G.dma_start(out=ropec[:, :].rearrange("p (t f) -> p t f", f=32),
                in_=ropec_d.ap().rearrange("(t p) f -> p t f", p=128),
                inc=("cGP", 16))
    G.dma_start(out=ropes[:, :].rearrange("p (t f) -> p t f", f=32),
                in_=ropes_d.ap().rearrange("(t p) f -> p t f", p=128),
                inc=("cGP", 16))

    for tt in range(NT):
        S.dma_start(out=xt[:, :], in_=t["h2seq"].ap()[tt * 128:(tt + 1) * 128, :],
                    wait=W("cDV", "qDV", "cAC", "qAC"), inc=("cLD", 16))
        V.reduce_sum(stat[:, 0:1], xt[:, :], axis=AX.X, negate=True, wait=W("cLD"))
        V.drain()
        V.tensor_scalar_mul(stat[:, 0:1], stat[:, 0:1], 1.0 / D)
        V.drain(inc=("cDV", 1))
        A.activation(qsa[:, :], xt[:, :], AF.Square, bias=stat[:, 0:1],
                     accum_out=stat[:, 1:2], wait=W("cDV"))
        A.drain()
        A.activation(stat[:, 3:4], stat[:, 1:2], AF.Ln, scale=1.0 / D, bias=EPS_LN)
        A.drain()
        A.activation(stat[:, 2:3], stat[:, 3:4], AF.Exp, scale=-0.5)
        A.drain(inc=("cAC", 1))
        V.tensor_scalar(ln_t[:, :], xt[:, :], stat[:, 0:1], stat[:, 2:3],
                        ALU.add, ALU.mult, wait=W("cAC", "cPE"), inc=("qDV", 1))
        emit_quant(p, ln_t[:, :], st_t[:, :], 128, qs)
        for kc in range(NKC):
            ptx = p_t[kc % 2]
            PE.transpose(ptx[:, :], st_t[:, kc * 128:(kc + 1) * 128],
                         ident[:, :], wait=W("qDV", "cDV"), inc=("cPE", 1))
            V.tensor_copy(xqfm[:, kc * 2048 + tt * 128:kc * 2048 + tt * 128 + 128],
                          ptx[:, :], wait=W("cPE"), inc=("cDV", 1))

    def wquant_T(w_d, nrows, ncols, dst_sb):
        nrt = nrows // 128
        nct = ncols // 128
        for rtile in range(nrt):
            G.dma_start(out=ln_t[:, 0:ncols],
                        in_=w_d.ap()[rtile * 128:(rtile + 1) * 128, :],
                        wait=W("cPE", "qDV", "qAC"), inc=("cGP", 16))
            V.tensor_copy(stat[0:1, 4:5], stat[0:1, 4:5], wait=W("cGP"),
                          inc=("qDV", 1))
            emit_quant(p, ln_t[:, 0:ncols], st_t[:, 0:ncols], ncols // 16, qs)
            for cc in range(nct):
                ptx = p_t[cc % 2]
                PE.transpose(ptx[:, :], st_t[:, cc * 128:(cc + 1) * 128],
                             ident[:, :], wait=W("qDV", "cDV"), inc=("cPE", 1))
                V.tensor_copy(
                    dst_sb[:, cc * nrows + rtile * 128:cc * nrows + rtile * 128 + 128],
                    ptx[:, :], wait=W("cPE"), inc=("cDV", 1))

    def projectC(which, w_d, out_d):
        wquant_T(w_d, HH, D, wq_sb)
        qt32 = bass.AP(ln_t, 0, [[2048, 128], [1, 1024]])
        rt32 = bass.AP(ln_t, 1024, [[2048, 128], [1, 1024]])
        sc32 = bass.AP(xt, 0, [[2048, 128], [1, 1024]])
        for tc in range(NT):
            for half in range(2):
                pb = p_s[half]
                for kc in range(NKC):
                    PE.matmul(
                        pb[:, :],
                        xqfm[:, kc * 2048 + tc * 128:kc * 2048 + tc * 128 + 128],
                        wq_sb[:, kc * 1024 + half * 512:kc * 1024 + half * 512 + 512],
                        start=(kc == 0), stop=(kc == NKC - 1),
                        wait=W("cDV") if kc == 0 else (),
                        inc=("cPE", 1) if kc == NKC - 1 else None)
                if which == "v":
                    V.tensor_copy(qt[:, half * 512:half * 512 + 512], pb[:, :],
                                  wait=W("cPE", "cST"), inc=("cDV", 1))
                else:
                    V.tensor_copy(qt32[:, half * 512:half * 512 + 512], pb[:, :],
                                  wait=W("cPE", "cST"), inc=("cDV", 1))
            if which == "v":
                S.dma_start(out=out_d.ap()[tc * 128:(tc + 1) * 128, :],
                            in_=qt[:, :], wait=W("cDV"), inc=("cST", 16))
                continue
            c2 = ropec[:, tc * 32:(tc + 1) * 32]
            s2 = ropes[:, tc * 32:(tc + 1) * 32]
            for h in range(16):
                b0 = h * 64
                lo, hi = qt32[:, b0:b0 + 32], qt32[:, b0 + 32:b0 + 64]
                rlo, rhi = rt32[:, b0:b0 + 32], rt32[:, b0 + 32:b0 + 64]
                V.tensor_mul(rlo, lo, c2, wait=W("cPE") if h == 0 else ())
                V.tensor_mul(sc32[:, 0:32], hi, s2)
                V.tensor_sub(rlo, rlo, sc32[:, 0:32])
                V.tensor_mul(rhi, hi, c2)
                V.tensor_mul(sc32[:, 0:32], lo, s2)
                V.tensor_add(rhi, rhi, sc32[:, 0:32])
            V.drain(inc=("cDV", 1))
            for hc in range(8):
                ptx = [p_tf, p_tf2][hc % 2]
                PE.transpose(ptx[:, :], rt32[:, hc * 128:(hc + 1) * 128],
                             identf[:, :], wait=W("cDV"), inc=("cPE", 1))
                V.tensor_copy(bass.AP(xt, hc * 128, [[2048, 128], [1, 128]]),
                              ptx[:, :], wait=W("cPE", "cST"), inc=("cDV", 1))
            S.dma_start(out=out_d.ap()[:, tc * 128:(tc + 1) * 128]
                        .rearrange("(hc p) tk -> p hc tk", p=128),
                        in_=bass.AP(xt, 0, [[2048, 128], [128, 8], [1, 128]]),
                        wait=W("cDV"), inc=("cST", 16))

    if stages >= 2:
        projectC("q", t["Wq"], t["qfm"])
        projectC("k", t["Wk"], t["kfm"])
        projectC("v", t["Wv"], t["vtm"])

    bank_guard = {0: 0, 1: 0}
    tbank_guard = {0: 0, 1: 0}
    for h in range(16 if stages >= 3 else 0):
        S.dma_start(out=q_h[0:64, :], in_=t["qfm"].ap()[h * 64:(h + 1) * 64, :],
                    wait=W("cST", "cPE", "cDV", "cAC"), inc=("cGP", 16))
        S.dma_start(out=k_h[0:64, :], in_=t["kfm"].ap()[h * 64:(h + 1) * 64, :],
                    inc=("cGP", 16))
        G.dma_start(out=v_h[:, 0:1024].rearrange("p (tkc dh) -> p tkc dh", dh=64),
                    in_=t["vtm"].ap()[:, h * 64:(h + 1) * 64]
                    .rearrange("(tkc p) dh -> p tkc dh", p=128),
                    wait=W("cST", "cPE", "cDV", "cAC"), inc=("cGP", 16))
        first_pe = True
        for q4 in range(4):
            V.memset(pt_buf[:, 0:(4 * (q4 + 1)) * 512], 0.0, wait=W("cPE"))
            for ti in range(4):
                i = q4 * 4 + ti
                tq0 = i * 128
                wdt = (i + 1) * 128
                nsl = (wdt + 511) // 512
                for sl in range(nsl):
                    n = min(512, wdt - sl * 512)
                    bk = sl % 2
                    PE.matmul(p_s[bk][:, 0:n],
                              q_h[0:64, tq0:tq0 + 128],
                              k_h[0:64, sl * 512:sl * 512 + n],
                              start=True, stop=True,
                              wait=(W("cGP") if first_pe else ()) +
                                   (("cAC", bank_guard[bk]),),
                              inc=("cPE", 1))
                    first_pe = False
                    A.activation(t_buf[:, sl * 512:sl * 512 + n], p_s[bk][:, 0:n],
                                 AF.Identity, scale=1.0 / 8.0,
                                 wait=W("cPE", "cDV"), inc=("cAC", 1))
                    bank_guard[bk] = p.cnt["cAC"]
                dj = wdt - 128
                V.tensor_mul(t_buf[:, dj:wdt], t_buf[:, dj:wdt], mask01[:, :],
                             wait=W("cAC"))
                V.scalar_tensor_tensor(t_buf[:, dj:wdt], moff[:, :], 3.0e38,
                                       t_buf[:, dj:wdt], ALU.mult, ALU.add)
                V.reduce_max(stat[:, 0:1], t_buf[:, 0:wdt], axis=AX.X,
                             negate=True)
                V.drain(inc=("cDV", 1))
                A.activation(e_buf[:, 0:wdt], t_buf[:, 0:wdt], AF.Exp,
                             bias=stat[:, 0:1], scale=1.0,
                             accum_out=stat[:, 1:2], wait=W("cDV"))
                A.drain(inc=("cAC", 1))
                V.reciprocal(stat[:, 2:3], stat[:, 1:2], wait=W("cAC"))
                V.drain()
                V.tensor_scalar(e_buf[:, 0:wdt], e_buf[:, 0:wdt],
                                stat[:, 2:3], None, ALU.mult, inc=("cDV", 1))
                for tkc in range(i + 1):
                    bk = tkc % 2
                    PE.transpose(p_t[bk][:, :], e_buf[:, tkc * 128:(tkc + 1) * 128],
                                 ident[:, :],
                                 wait=W("cDV") + (("cDV", tbank_guard[bk]),),
                                 inc=("cPE", 1))
                    V.tensor_copy(pt_buf[:, tkc * 512 + ti * 128:
                                         tkc * 512 + ti * 128 + 128],
                                  p_t[bk][:, :], wait=W("cPE"), inc=("cDV", 1))
                    tbank_guard[bk] = p.cnt["cDV"]
            ntk = 4 * (q4 + 1)
            for tkc in range(ntk):
                PE.matmul(p_o[0:64, :],
                          v_h[:, tkc * 64:(tkc + 1) * 64],
                          pt_buf[:, tkc * 512:(tkc + 1) * 512],
                          start=(tkc == 0), stop=(tkc == ntk - 1),
                          wait=W("cDV", "cAC") if tkc == 0 else (),
                          inc=("cPE", 1) if tkc == ntk - 1 else None)
            A.activation(oh32[0:64, q4 * 512:q4 * 512 + 512],
                         p_o[0:64, :], AF.Identity, wait=W("cPE"), inc=("cAC", 1))
        ch, base = h // 2, (h % 2) * 64
        oth = bass.AP(st_t.bitcast(F32), 0, [[1024, 128], [1, 1024]])
        oqh = bass.AP(qt, 0, [[1024, 128], [1, 1024]])
        for tc2 in range(NT):
            PE.transpose(p_tf[0:128, 0:64], oh32[0:64, tc2 * 128:(tc2 + 1) * 128],
                         identf[0:64, 0:64], wait=W("cAC", "cDV", "qDV"), inc=("cPE", 1))
            V.tensor_copy(bass.AP(st_t.bitcast(F32), tc2 * 64, [[1024, 128], [1, 64]]), p_tf[0:128, 0:64],
                          wait=W("cPE"), inc=("qDV", 1))
        emit_quant(p, oth, oqh, 64, qs)
        for tc2 in range(NT):
            bk2 = tc2 % 2
            PE.transpose(p_t[bk2][0:64, 0:128], bass.AP(qt, tc2 * 64, [[1024, 128], [1, 64]]),
                         ident[:, :], wait=W("qDV", "cDV"), inc=("cPE", 1))
            V.tensor_copy(o_sb[base:base + 64,
                               ch * 2048 + tc2 * 128:ch * 2048 + tc2 * 128 + 128],
                          p_t[bk2][0:64, 0:128], wait=W("cPE"), inc=("cDV", 1))

    if debug:
        for kc in range(8):
            S.dma_start(out=t["dbg_o"].ap()[kc * 128:(kc + 1) * 128, :],
                        in_=o_sb[:, kc * 2048:(kc + 1) * 2048],
                        wait=W("cDV", "cAC"), inc=("cST", 16))
        G.dma_start(out=t["dbg_oh"].ap()[0:128, :],
                    in_=bass.AP(oh32, 0, [[2048, 128], [1, 1024]]),
                    wait=W("cDV", "cAC"), inc=("cGP", 16))
        G.dma_start(out=t["dbg_oh"].ap()[128:256, :],
                    in_=bass.AP(st_t.bitcast(F32), 0, [[1024, 128], [1, 1024]]),
                    wait=W("cDV", "cAC"), inc=("cGP", 16))
        G.dma_start(out=t["dbg_q2"].ap()[0:128, :], in_=qt[:, :],
                    wait=W("cDV", "cAC"), inc=("cGP", 16))
        G.dma_start(out=t["dbg_t"].ap()[0:128, :], in_=t_buf,
                    wait=W("cDV", "cAC"), inc=("cGP", 16))
        G.dma_start(out=t["dbg_e"].ap()[0:128, :], in_=e_buf,
                    wait=W("cDV", "cAC"), inc=("cGP", 16))
        G.dma_start(out=t["dbg_pt"].ap()[0:128, :], in_=pt_buf,
                    wait=W("cDV", "cAC"), inc=("cGP", 16))
        G.dma_start(out=t["dbg_st"].ap()[0:1, :].bitcast(F32), in_=stat[0:1, :],
                    wait=W("cDV", "cAC"), inc=("cGP", 16))
        G.dma_start(out=t["dbg_v"].ap()[0:128, :], in_=v_h[:, 0:1024],
                    wait=W("cDV", "cAC"), inc=("cGP", 16))
        G.dma_start(out=t["dbg_ohraw"].ap()[0:64, :],
                    in_=bass.AP(oh32, 0, [[2048, 64], [1, 2048]]),
                    wait=W("cDV", "cAC"), inc=("cGP", 16))
    if stages >= 4:
        wquant_T(t["Wo"], D, HH, wq_sb)
    for tc in range(NT if stages >= 4 else 0):
        for dsl in range(4):
            pb = p_s[dsl % 2]
            for kc in range(8):
                PE.matmul(pb[:, :],
                          o_sb[:, kc * 2048 + tc * 128:kc * 2048 + tc * 128 + 128],
                          wq_sb[:, kc * 2048 + dsl * 512:kc * 2048 + dsl * 512 + 512],
                          start=(kc == 0), stop=(kc == 7),
                          wait=W("cGP", "cAC", "cDV") if kc == 0 else (),
                          inc=("cPE", 1) if kc == 7 else None)
            V.tensor_copy(st_t[:, dsl * 512:dsl * 512 + 512], pb[:, :],
                          wait=W("cPE", "cST"), inc=("cDV", 1))
        S.dma_start(out=t["dC"].ap()[tc * 128:(tc + 1) * 128, :],
                    in_=st_t[:, :], wait=W("cDV"), inc=("cST", 16))
    S.raw(lambda s: s.wait_ge(p.sem["cST"], p.cnt["cST"]))


def build_C(debug=False, stages=9):
    nc = bass.Bass("TRN2", target_bir_lowering=False)
    t = {}
    t["h2seq"] = nc.dram_tensor("h2seq", [T, D], F32, kind="ExternalInput")
    t["Wq"] = nc.dram_tensor("Wq", [HH, D], F32, kind="ExternalInput")
    t["Wk"] = nc.dram_tensor("Wk", [HH, D], F32, kind="ExternalInput")
    t["Wv"] = nc.dram_tensor("Wv", [HH, D], F32, kind="ExternalInput")
    t["Wo"] = nc.dram_tensor("Wo", [D, HH], F32, kind="ExternalInput")
    kind = "ExternalOutput" if debug else "Internal"
    t["qfm"] = nc.dram_tensor("qfm", [HH, T], F32, kind=kind)
    t["kfm"] = nc.dram_tensor("kfm", [HH, T], F32, kind=kind)
    t["vtm"] = nc.dram_tensor("vtm", [T, HH], BF16, kind=kind)
    t["dC"] = nc.dram_tensor("dC", [T, D], BF16, kind="ExternalOutput")
    if debug:
        t["dbg_o"] = nc.dram_tensor("dbg_o", [1024, T], BF16, kind="ExternalOutput")
        t["dbg_oh"] = nc.dram_tensor("dbg_oh", [256, 1024], F32, kind="ExternalOutput")
        t["dbg_q2"] = nc.dram_tensor("dbg_q2", [128, 1024], BF16, kind="ExternalOutput")
        t["dbg_t"] = nc.dram_tensor("dbg_t", [128, 2048], F32, kind="ExternalOutput")
        t["dbg_e"] = nc.dram_tensor("dbg_e", [128, 2048], BF16, kind="ExternalOutput")
        t["dbg_pt"] = nc.dram_tensor("dbg_pt", [128, 8192], BF16, kind="ExternalOutput")
        t["dbg_st"] = nc.dram_tensor("dbg_st", [1, 32], BF16, kind="ExternalOutput")
        t["dbg_v"] = nc.dram_tensor("dbg_v", [128, 1024], BF16, kind="ExternalOutput")
        t["dbg_ohraw"] = nc.dram_tensor("dbg_ohraw", [64, 2048], F32, kind="ExternalOutput")

    with ExitStack() as ctx:
        block = ctx.enter_context(nc.Block())
        p = Prog(nc, block, ctx)
        emit_phaseC(p, t, debug=debug, stages=stages)
    return nc


def phaseC_inmaps(inputs, h2_full):
    maps = []
    for c in range(8):
        s, h = c // 2, c % 2
        maps.append({
            "h2seq": np.ascontiguousarray(h2_full[s], dtype=np.float32),
            "Wq": np.ascontiguousarray(inputs["L1_attn_Wq"][h * HH:(h + 1) * HH, :]),
            "Wk": np.ascontiguousarray(inputs["L1_attn_Wk"][h * HH:(h + 1) * HH, :]),
            "Wv": np.ascontiguousarray(inputs["L1_attn_Wv"][h * HH:(h + 1) * HH, :]),
            "Wo": np.ascontiguousarray(inputs["L1_attn_Wo"][:, h * HH:(h + 1) * HH]),
        })
    return maps



def emit_phaseD(p, t, debug=False):
    nc = p.nc
    ctx = p.ctx
    S, G, V, A, PE = engines(p)
    E = ctx.enter_context
    W = p.W

    ln3 = E(nc.sbuf_tensor("d_ln3", [128, 16 * TOK], BF16))
    fuse = E(nc.sbuf_tensor("d_fuse", [128, 64 * 512], BF16))
    wgb = E(nc.sbuf_tensor("d_wgb", [128, 16 * 512], BF16))
    wub = E(nc.sbuf_tensor("d_wub", [128, 16 * 512], BF16))
    wdb = E(nc.sbuf_tensor("d_wdb", [128, 8 * 1024], BF16))
    xt = E(nc.sbuf_tensor("d_xt", [128, 2048], F32))
    dc0 = E(nc.sbuf_tensor("d_dc0", [128, 2048], BF16))
    dc1 = E(nc.sbuf_tensor("d_dc1", [128, 2048], BF16))
    h3t = E(nc.sbuf_tensor("d_h3t", [128, 2048], F32))
    ln_t = E(nc.sbuf_tensor("d_lnt", [128, 2048], BF16))
    sq_t = E(nc.sbuf_tensor("d_sqt", [128, 2048], BF16))
    tg = E(nc.sbuf_tensor("d_tg", [128, 512], BF16))
    tu = E(nc.sbuf_tensor("d_tu", [128, 512], BF16))
    sgb = E(nc.sbuf_tensor("d_sgb", [128, 64], F32))
    sub = E(nc.sbuf_tensor("d_sub", [128, 64], F32))
    sdrow = E(nc.sbuf_tensor("d_sdrow", [1, 2048], BF16))
    sdb = E(nc.sbuf_tensor("d_sdb", [128, 2048], BF16))
    onec = E(nc.sbuf_tensor("d_onec", [1, 128], BF16))
    ident = E(nc.sbuf_tensor("d_ident", [128, 128], BF16))
    stat = E(nc.sbuf_tensor("d_stat", [128, 16], F32))

    pA = E(nc.psum_tensor("d_pA", [128, 512], F32))
    pB = E(nc.psum_tensor("d_pB", [128, 512], F32))
    pT = [E(nc.psum_tensor("d_pT0", [128, 128], BF16)),
          E(nc.psum_tensor("d_pT1", [128, 128], BF16))]
    pD = [E(nc.psum_tensor(f"d_pD{i}", [128, 512], F32)) for i in range(4)]

    for nm in ["dLD", "dPE", "dDV", "dAC", "dST", "dGP"]:
        p.new_sem(nm)

    ident_d = nc.inline_tensor(np.eye(128, dtype=bf), "identD")
    onec_d = nc.inline_tensor(np.ones((1, 128), dtype=bf), "onecD")
    S.dma_start(out=ident[:, :], in_=ident_d.ap(), inc=("dLD", 16))
    S.dma_start(out=onec[:, :], in_=onec_d.ap(), inc=("dLD", 16))
    G.dma_start(out=sgb[:, :], in_=t["sg"].ap(), inc=("dGP", 16))
    G.dma_start(out=sub[:, :], in_=t["su"].ap(), inc=("dGP", 16))
    G.dma_start(out=sdrow[0:1, :], in_=t["sd"].ap(), inc=("dGP", 16))
    for q4 in range(4):
        PE.matmul(pA[:, :], onec[0:1, :], sdrow[0:1, q4 * 512:(q4 + 1) * 512],
                  start=True, stop=True, wait=W("dGP", "dLD", "dDV"), inc=("dPE", 1))
        V.tensor_copy(sdb[:, q4 * 512:(q4 + 1) * 512], pA[:, :],
                      wait=W("dPE"), inc=("dDV", 1))

    for tc in range(TC):
        S.dma_start(out=xt[:, :], in_=t["h2sl"].ap()[tc * 128:(tc + 1) * 128, :],
                    wait=W("dDV", "dAC"), inc=("dLD", 16))
        S.dma_start(out=dc0[:, :], in_=t["dC0"].ap()[tc * 128:(tc + 1) * 128, :],
                    inc=("dLD", 16))
        S.dma_start(out=dc1[:, :], in_=t["dC1"].ap()[tc * 128:(tc + 1) * 128, :],
                    inc=("dLD", 16))
        V.tensor_add(h3t[:, :], xt[:, :], dc0[:, :], wait=W("dLD"))
        V.tensor_add(h3t[:, :], h3t[:, :], dc1[:, :], inc=("dDV", 1))
        S.dma_start(out=t["h3d"].ap()[tc * 128:(tc + 1) * 128, :],
                    in_=h3t[:, :], wait=W("dDV"), inc=("dST", 16))
        V.reduce_sum(stat[:, 0:1], h3t[:, :], axis=AX.X, negate=True)
        V.drain()
        V.tensor_scalar_mul(stat[:, 0:1], stat[:, 0:1], 1.0 / D)
        V.drain(inc=("dDV", 1))
        A.activation(sq_t[:, :], h3t[:, :], AF.Square, bias=stat[:, 0:1],
                     accum_out=stat[:, 1:2], wait=W("dDV"))
        A.drain()
        A.activation(stat[:, 3:4], stat[:, 1:2], AF.Ln, scale=1.0 / D, bias=EPS_LN)
        A.drain()
        A.activation(stat[:, 2:3], stat[:, 3:4], AF.Exp, scale=-0.5)
        A.drain(inc=("dAC", 1))
        V.tensor_scalar(ln_t[:, :], h3t[:, :], stat[:, 0:1], stat[:, 2:3],
                        ALU.add, ALU.mult, wait=W("dAC", "dPE"), inc=("dDV", 1))
        for kc in range(NKC):
            ptx = pT[kc % 2]
            PE.transpose(ptx[:, :], ln_t[:, kc * 128:(kc + 1) * 128],
                         ident[:, :], wait=W("dDV"), inc=("dPE", 1))
            V.tensor_copy(ln3[:, kc * TOK + tc * 128:kc * TOK + tc * 128 + 128],
                          ptx[:, :], wait=W("dPE"), inc=("dDV", 1))

    for th in range(2):
        for og in range(16):
            G.dma_start(out=wgb[:, :].rearrange("p (kc f) -> p kc f", f=512),
                        in_=t["wg"].ap()[:, og * 512:(og + 1) * 512]
                        .rearrange("(kc p) f -> p kc f", p=128),
                        wait=W("dPE", "dDV", "dAC"), inc=("dGP", 16))
            G.dma_start(out=wub[:, :].rearrange("p (kc f) -> p kc f", f=512),
                        in_=t["wu"].ap()[:, og * 512:(og + 1) * 512]
                        .rearrange("(kc p) f -> p kc f", p=128),
                        inc=("dGP", 16))
            for oc in range(4):
                okc = og * 4 + oc
                for kc in range(NKC):
                    PE.matmul(pA[:, :],
                              wgb[:, kc * 512 + oc * 128:kc * 512 + oc * 128 + 128],
                              ln3[:, kc * TOK + th * 512:kc * TOK + th * 512 + 512],
                              start=(kc == 0), stop=(kc == NKC - 1),
                              wait=W("dGP", "dDV", "dAC") if (oc == 0 and kc == 0) else
                                   (W("dAC") if kc == 0 else ()),
                              inc=("dPE", 1) if kc == NKC - 1 else None)
                A.activation(tg[:, :], pA[:, :], AF.Silu,
                             scale=sgb[:, okc:okc + 1], wait=W("dPE", "dDV"),
                             inc=("dAC", 1))
                for kc in range(NKC):
                    PE.matmul(pB[:, :],
                              wub[:, kc * 512 + oc * 128:kc * 512 + oc * 128 + 128],
                              ln3[:, kc * TOK + th * 512:kc * TOK + th * 512 + 512],
                              start=(kc == 0), stop=(kc == NKC - 1),
                              wait=W("dDV") if kc == 0 else (),
                              inc=("dPE", 1) if kc == NKC - 1 else None)
                V.tensor_scalar(tu[:, :], pB[:, :], sub[:, okc:okc + 1], None,
                                ALU.mult, wait=W("dPE", "dAC"))
                V.tensor_mul(fuse[:, okc * 512:(okc + 1) * 512], tg[:, :], tu[:, :],
                             inc=("dDV", 1))
        for dsl in range(4):
            for og2 in range(8):
                G.dma_start(out=wdb[:, 0:8 * 512].rearrange("p (kc f) -> p kc f", f=512),
                            in_=t["wd"].ap()[og2 * 1024:(og2 + 1) * 1024,
                                             dsl * 512:(dsl + 1) * 512]
                            .rearrange("(kc p) f -> p kc f", p=128),
                            wait=W("dPE") if (og2 > 0) else W("dPE", "dDV"),
                            inc=("dGP", 16))
                for tc2 in range(4):
                    for kc in range(8):
                        PE.matmul(
                            pD[tc2][:, :],
                            fuse[:, (og2 * 8 + kc) * 512 + tc2 * 128:
                                 (og2 * 8 + kc) * 512 + tc2 * 128 + 128],
                            wdb[:, kc * 512:kc * 512 + 512],
                            start=(og2 == 0 and kc == 0),
                            stop=(og2 == 7 and kc == 7),
                            wait=W("dGP", "dDV") if kc == 0 else (),
                            inc=("dPE", 1) if kc == 7 else None)
            for tc2 in range(4):
                row0 = th * 512 + tc2 * 128
                S.dma_start(out=h3t[:, 0:512],
                            in_=t["h3d"].ap()[row0:row0 + 128,
                                              dsl * 512:(dsl + 1) * 512],
                            wait=W("dDV", "dST"), inc=("dLD", 16))
                V.tensor_mul(xt[:, 0:512], pD[tc2][:, :],
                             sdb[:, dsl * 512:(dsl + 1) * 512],
                             wait=W("dPE", "dLD"))
                V.tensor_add(xt[:, 0:512], xt[:, 0:512], h3t[:, 0:512],
                             inc=("dDV", 1))
                S.dma_start(out=t["outsl"].ap()[row0:row0 + 128,
                                                dsl * 512:(dsl + 1) * 512],
                            in_=xt[:, 0:512], wait=W("dDV"), inc=("dST", 16))

    S.raw(lambda s: s.wait_ge(p.sem["dST"], p.cnt["dST"]))


def build_D(debug=False):
    nc = bass.Bass("TRN2", target_bir_lowering=False)
    t = {}
    t["h2sl"] = nc.dram_tensor("h2sl", [TOK, D], F32, kind="ExternalInput")
    t["dC0"] = nc.dram_tensor("dC0", [TOK, D], BF16, kind="ExternalInput")
    t["dC1"] = nc.dram_tensor("dC1", [TOK, D], BF16, kind="ExternalInput")
    t["wg"] = nc.dram_tensor("wg", [D, 8192], I8, kind="ExternalInput")
    t["wu"] = nc.dram_tensor("wu", [D, 8192], I8, kind="ExternalInput")
    t["wd"] = nc.dram_tensor("wd", [8192, D], I8, kind="ExternalInput")
    t["sg"] = nc.dram_tensor("sg", [128, 64], F32, kind="ExternalInput")
    t["su"] = nc.dram_tensor("su", [128, 64], F32, kind="ExternalInput")
    t["sd"] = nc.dram_tensor("sd", [1, D], F32, kind="ExternalInput")
    t["h3d"] = nc.dram_tensor("h3d", [TOK, D], F32)
    t["outsl"] = nc.dram_tensor("outsl", [TOK, D], F32, kind="ExternalOutput")

    with ExitStack() as ctx:
        block = ctx.enter_context(nc.Block())
        p = Prog(nc, block, ctx)
        emit_phaseD(p, t, debug=debug)
    return nc


def phaseD_inmaps(inputs, h2_by_core, dC_by_core):
    maps = []
    for c in range(8):
        s, h = c // 2, c % 2
        rows = slice(h * TOK, (h + 1) * TOK)
        maps.append({
            "h2sl": np.ascontiguousarray(h2_by_core[c], dtype=np.float32),
            "dC0": np.ascontiguousarray(dC_by_core[2 * s][rows]),
            "dC1": np.ascontiguousarray(dC_by_core[2 * s + 1][rows]),
            "wg": inputs["L1_ffn_w_gate"], "wu": inputs["L1_ffn_w_up"],
            "wd": inputs["L1_ffn_w_down"],
            "sg": np.ascontiguousarray(
                np.asarray(inputs["L1_ffn_s_gate"], dtype=np.float32).reshape(64, 128).T),
            "su": np.ascontiguousarray(
                np.asarray(inputs["L1_ffn_s_up"], dtype=np.float32).reshape(64, 128).T),
            "sd": np.ascontiguousarray(inputs["L1_ffn_s_down"][None, :], dtype=np.float32),
        })
    return maps



def kernel(**inputs):
    cores = list(range(8))

    ncA = build_A()
    mapsA = phaseA_inmaps(inputs)
    resA = run_bass_kernel_spmd(ncA, mapsA, core_ids=cores)
    dA = [resA.results[c]["dA"] for c in cores]

    ncB = build_B()
    mapsB = phaseB_inmaps(inputs, dA)
    resB = run_bass_kernel_spmd(ncB, mapsB, core_ids=cores)
    h2 = [resB.results[c]["h2"] for c in cores]

    h2_full = np.stack([np.concatenate([h2[2 * s], h2[2 * s + 1]], axis=0)
                        for s in range(4)])
    ncC = build_C()
    mapsC = phaseC_inmaps(inputs, h2_full)
    resC = run_bass_kernel_spmd(ncC, mapsC, core_ids=cores)
    dC = [resC.results[c]["dC"] for c in cores]

    ncD = build_D()
    mapsD = phaseD_inmaps(inputs, h2, dC)
    resD = run_bass_kernel_spmd(ncD, mapsD, core_ids=cores)
    out = np.stack([np.concatenate([resD.results[2 * s]["outsl"],
                                    resD.results[2 * s + 1]["outsl"]], axis=0)
                    for s in range(4)])
    return out.astype(np.float32)
